# version 01 (final)
# baseline (speedup 1.0000x reference)
import contextlib
import math

import numpy as np
import ml_dtypes

import concourse.bass as bass
import concourse.mybir as mybir
from concourse.alu_op_type import AluOpType as OP
from concourse.bass_utils import run_bass_kernel_spmd

AF = mybir.ActivationFunctionType
F32 = mybir.dt.float32
BF16 = mybir.dt.bfloat16

D = 1024
S = 2048
NSEQ = 2
T = NSEQ * S
NCH = D // 128
L = 128
W = 1024
F = 4096
E = 8
I = 2048
G = 32
DG = D // G
EPS_OUT = 1e-5
EPS_IN = 1e-6
MAGIC = float(1.5 * 2 ** 23)
TWO_PI = float(2.0 * math.pi)
INV_2PI = float(1.0 / (2.0 * math.pi))
PI_CLAMP = 3.14159

FFN_TG = 1024
MOE_TG = 2048


class Sem:
    def __init__(self, stack, nc, name):
        self.h = stack.enter_context(nc.semaphore(name))
        self.v = 0

    def inc(self, instr, n=1):
        instr.then_inc(self.h, n)
        self.v += n
        return (self, self.v)

    def tok(self):
        return (self, self.v)


class Builder:
    def __init__(self, debug=()):
        self.debug = set(debug)
        self.nc = bass.Bass("TRN2", target_bir_lowering=False)
        self.stack = contextlib.ExitStack()
        self.waited = {}
        self.dbg_tensors = {}

    def wait(self, eng, tok):
        if tok is None:
            return
        sem, val = tok
        if val <= 0:
            return
        key = (id(eng), id(sem))
        if self.waited.get(key, -1) >= val:
            return
        self.waited[key] = val
        eng.wait_ge(sem.h, val)

    def V(self, method, *a, dr=True, **kw):
        ins = getattr(self.nc.vector, method)(*a, **kw)
        if dr:
            self.nc.vector.drain()
        return ins

    def A(self, method, *a, dr=True, **kw):
        ins = getattr(self.nc.scalar, method)(*a, **kw)
        if dr:
            self.nc.scalar.drain()
        return ins

    PROBE_C = 0

    SCN = {"eps_out": 0, "eps_in": 1, "neg_half": 2, "neg_one": 3, "tiny": 4,
           "quarter": 5, "inv2pi": 6, "half": 7}

    def SC(self, name):
        i = self.SCN[name]
        return self.c_sconst[:, i:i + 1]

    def dram(self, name, shape, dtype, kind):
        return self.nc.dram_tensor(name, shape, dtype, kind=kind).ap()

    def sb(self, name, shape, dtype):
        return self.stack.enter_context(self.nc.sbuf_tensor(name, shape, dtype))

    def ps(self, name, shape, dtype):
        return self.stack.enter_context(self.nc.psum_tensor(name, shape, dtype))

    def dbg(self, name, shape, dtype=F32):
        if name in self.debug:
            if name not in self.dbg_tensors:
                self.dbg_tensors[name] = self.dram("dbg_" + name, shape, dtype, "ExternalOutput")
            return self.dbg_tensors[name]
        return None

    def declare(self):
        nc = self.nc
        d = self.dram
        self.x = d("x", [T, D], F32, "ExternalInput")
        self.Wq = d("Wq", [D, D], F32, "ExternalInput")
        self.Wk = d("Wk", [D, D], F32, "ExternalInput")
        self.Wv = d("Wv", [D, D], F32, "ExternalInput")
        self.Wa = d("Wa", [D, 2 * D], F32, "ExternalInput")
        self.Wg = d("Wg", [D, D], F32, "ExternalInput")
        self.Wo = d("Wo", [D, D], F32, "ExternalInput")
        self.W1 = d("W1", [D, F], F32, "ExternalInput")
        self.W2 = d("W2", [F, D], F32, "ExternalInput")
        self.b1 = d("b1", [F], F32, "ExternalInput")
        self.b2 = d("b2", [D], F32, "ExternalInput")
        self.gns = d("gns", [D], F32, "ExternalInput")
        self.gnb = d("gnb", [D], F32, "ExternalInput")
        self.ln1s = d("ln1s", [D], F32, "ExternalInput")
        self.ln1b = d("ln1b", [D], F32, "ExternalInput")
        self.ln2s = d("ln2s", [D], F32, "ExternalInput")
        self.ln2b = d("ln2b", [D], F32, "ExternalInput")
        self.rw = d("rw", [E, D], F32, "ExternalInput")
        self.w13 = d("w13", [E, D, 2 * I], F32, "ExternalInput")
        self.w2m = d("w2m", [E, I, D], F32, "ExternalInput")
        self.identb = d("identb", [128, 128], BF16, "ExternalInput")
        self.identf = d("identf", [128, 128], F32, "ExternalInput")
        self.onesrow = d("onesrow", [1, 128], F32, "ExternalInput")
        self.onesrowb = d("onesrowb", [1, 128], BF16, "ExternalInput")
        self.onescol = d("onescol", [128, 2], BF16, "ExternalInput")
        self.cmask = d("cmask", [128, W], F32, "ExternalInput")
        self.g4ind = d("g4ind", [128, 4], F32, "ExternalInput")
        self.g4exp = d("g4exp", [4, 128], F32, "ExternalInput")
        self.sconst = d("sconst", [128, 8], F32, "ExternalInput")
        self.out = d("out", [T, D], F32, "ExternalOutput")
        self.xn_scr = d("xn_scr", [T, D], BF16, "Internal")
        self.t_scr = d("t_scr", [D, T], BF16, "Internal")
        self.attn_scr = d("attn_scr", [D, T], BF16, "Internal")
        self.s_scr = d("s_scr", [E, T], BF16, "Internal")

        st = self.stack
        self.dX = Sem(st, nc, "dX")
        self.dW = Sem(st, nc, "dW")
        self.dT = Sem(st, nc, "dT")
        self.dS = Sem(st, nc, "dS")
        self.dO = Sem(st, nc, "dO")
        self.sT = Sem(st, nc, "sT")
        self.sV = Sem(st, nc, "sV")
        self.sA = Sem(st, nc, "sA")

        self.pb = [self.ps(f"pb{i}", [128, 512], F32) for i in range(7)]
        self.pt = self.ps("pt", [128, 1024], BF16)
        self.slot_free = [None] * 8

        self.c_identb = self.sb("c_identb", [128, 128], BF16)
        self.c_identf = self.sb("c_identf", [128, 128], F32)
        self.c_onesrow = self.sb("c_onesrow", [1, 128], F32)
        self.c_onesrowb = self.sb("c_onesrowb", [1, 128], BF16)
        self.c_onescol = self.sb("c_onescol", [128, 2], BF16)
        self.c_cmask = self.sb("c_cmask", [128, W], F32)
        self.c_g4ind = self.sb("c_g4ind", [128, 4], F32)
        self.c_g4exp = self.sb("c_g4exp", [4, 128], F32)
        self.c_sconst = self.sb("c_sconst", [128, 8], F32)
        self.p_gns = self.sb("p_gns", [128, NCH], F32)
        self.p_gnb = self.sb("p_gnb", [128, NCH], F32)
        self.p_ln1s = self.sb("p_ln1s", [128, NCH], F32)
        self.p_ln1b = self.sb("p_ln1b", [128, NCH], F32)
        self.p_ln2s = self.sb("p_ln2s", [128, NCH], F32)
        self.p_ln2b = self.sb("p_ln2b", [128, NCH], F32)
        self.p_b1 = self.sb("p_b1", [128, F // 128], F32)
        self.p_b2 = self.sb("p_b2", [128, NCH], F32)
        self.ln_scr = None
        self.lnout_tok = [None, None]
        self.gntmp = self.sb("gntmp", [128, 6], F32)
        self.rwt = [self.sb(f"rwt{k}", [128, E], BF16) for k in range(NCH)]
        self.rw_sb = self.sb("rw_sb", [E, D], BF16)
        self.r_e = self.sb("r_e", [128, E], F32)
        self.r_eq = self.sb("r_eq", [128, E], F32)
        self.r_s = self.sb("r_s", [128, E], F32)
        self.r_st = self.sb("r_st", [128, 8], F32)
        self.r_sT = self.sb("r_sT", [E, 128], BF16)

    def load_constants(self):
        nc = self.nc
        sy = nc.sync
        gp = nc.gpsimd
        for dst, src in [
            (self.c_identb, self.identb), (self.c_identf, self.identf),
            (self.c_onesrow, self.onesrow), (self.c_onesrowb, self.onesrowb),
            (self.c_onescol, self.onescol),
            (self.c_cmask, self.cmask), (self.c_g4ind, self.g4ind),
            (self.c_g4exp, self.g4exp), (self.c_sconst, self.sconst),
        ]:
            self.dX.inc(sy.dma_start(out=dst[:], in_=src), 16)
        for dst, src, n in [
            (self.p_gns, self.gns, NCH), (self.p_gnb, self.gnb, NCH),
            (self.p_ln1s, self.ln1s, NCH), (self.p_ln1b, self.ln1b, NCH),
            (self.p_ln2s, self.ln2s, NCH), (self.p_ln2b, self.ln2b, NCH),
            (self.p_b1, self.b1, F // 128), (self.p_b2, self.b2, NCH),
        ]:
            self.dX.inc(sy.dma_start(out=dst[:], in_=src.rearrange("(j p) -> p j", p=128),
                                     allow_slow_non_contiguous=True), 16)
        self.const_tok = self.dX.tok()

    def p0_outer_ln(self):
        nc = self.nc
        v, a, sy = nc.vector, nc.scalar, nc.sync
        NB = 4
        self._p0_stack = contextlib.ExitStack()
        psb = lambda n, s, t: self._p0_stack.enter_context(nc.sbuf_tensor(n, s, t))
        xt = [psb(f"p0_x{i}", [128, D], F32) for i in range(NB)]
        xnt = [psb(f"p0_xn{i}", [128, D], BF16) for i in range(NB)]
        dump = psb("p0_dump", [128, D], BF16)
        stats = psb("p0_stats", [128, 8], F32)
        ntile = T // 128
        load_toks = [None] * ntile
        store_toks = [None] * ntile
        apply_toks = [None] * ntile
        for i in range(ntile + NB):
            if i < ntile:
                b = i % NB
                self.wait(sy, apply_toks[i - NB] if i >= NB else None)
                self.wait(sy, store_toks[i - NB] if i >= NB else None)
                load_toks[i] = self.dX.inc(
                    sy.dma_start(out=xt[b][:], in_=self.x[i * 128:(i + 1) * 128, :]), 16)
            j = i - 1
            if 0 <= j < ntile:
                b = j % NB
                self.wait(v, load_toks[j])
                self.V("tensor_reduce", out=stats[:, 0:1], in_=xt[b][:],
                       axis=mybir.AxisListType.X, op=OP.add)
                self.wait(a, load_toks[j])
                sa = self.sA.inc(self.A("activation", out=dump[:], in_=xt[b][:], func=AF.Square,
                                        accum_out=stats[:, 1:2]))
                self.V("tensor_scalar_mul", out=stats[:, 2:3], in0=stats[:, 0:1], scalar1=1.0 / D)
                self.V("tensor_tensor", out=stats[:, 3:4], in0=stats[:, 2:3], in1=stats[:, 2:3], op=OP.mult)
                self.wait(v, sa)
                sv = self.sV.inc(
                    self.V("scalar_tensor_tensor", out=stats[:, 4:5], in0=stats[:, 1:2], scalar=1.0 / D,
                           in1=stats[:, 3:4], op0=OP.mult, op1=OP.subtract))
                self.wait(a, sv)
                self.A("activation", out=stats[:, 5:6], in_=stats[:, 4:5], func=AF.Ln,
                       bias=self.SC("eps_out"))
                sa2 = self.sA.inc(
                    self.A("activation", out=stats[:, 6:7], in_=stats[:, 5:6], func=AF.Exp,
                           scale=self.SC("neg_half")))
                self.wait(v, sa2)
                apply_toks[j] = self.sV.inc(
                    self.V("tensor_scalar", out=xnt[b][:], in0=xt[b][:], scalar1=stats[:, 2:3],
                           scalar2=stats[:, 6:7], op0=OP.subtract, op1=OP.mult))
                self.wait(sy, apply_toks[j])
                store_toks[j] = self.dS.inc(
                    sy.dma_start(out=self.xn_scr[j * 128:(j + 1) * 128, :], in_=xnt[b][:]), 16)
        self.p0_done = self.dS.tok()
        self._p0_stack.close()
        dbg = self.dbg("xn", [T, D], BF16)
        if dbg is not None:
            self.wait(sy, self.p0_done)
            self.dbg_done = self.dO.inc(sy.dma_start(out=dbg, in_=self.xn_scr), 16)

    def alloc_sigma_bufs(self):
        self._sig_stack = contextlib.ExitStack()
        ssb = lambda n, s, t: self._sig_stack.enter_context(self.nc.sbuf_tensor(n, s, t))
        self.xnfm = [ssb(f"xnfm{k}", [128, S], BF16) for k in range(NCH)]
        self.yr_all = [ssb(f"yra{c}", [128, S], BF16) for c in range(NCH)]
        self.gnstats = [ssb(f"gnst{s_}", [128, 3 * NCH + 4], F32) for s_ in range(NSEQ)]
        self.wsl = [[ssb(f"wsl{b}_{t}", [128, 8 * 128], BF16) for t in range(5)]
                    for b in range(2)]
        self.q_s = ssb("q_s", [128, W], BF16)
        self.yi_buf = ssb("yi_buf", [128, W], BF16)
        self.dump_bf = ssb("dump_bf", [128, W], BF16)
        self.pool = [ssb(f"pp{i}", [128, W], F32) for i in range(9)]
        self.hinr = ssb("hinr", [128, 10], F32)
        self.hini = ssb("hini", [128, 10], F32)
        self.hseed = ssb("hseed", [128, 4], F32)
        self.pend = ssb("pend", [128, 18], F32)
        self.slot_free = [None] * 8
        self.xnfm_free = []
        self.wsl_free = [None, None]
        self._ssb = ssb

    def free_sigma_bufs(self):
        self._sig_stack.close()

    def _ends(self, buf, nchunk=W // L):
        return buf[:].rearrange("p (c l) -> p c l", l=L)[:, :, L - 1:L]

    def _bc(self, tile, ncol, nchunk=W // L):
        return tile[:, 0:nchunk].rearrange("p (c o) -> p c o", o=1).broadcast_to((128, nchunk, L))

    def _c3(self, buf):
        return buf[:].rearrange("p (c l) -> p c l", l=L)

    def p1_scan(self, sig):
        nc = self.nc
        v, a, te, sy, gp = nc.vector, nc.scalar, nc.tensor, nc.sync, nc.gpsimd
        P = self.pool
        probe_on = ("probe" in self.debug)
        if probe_on and not hasattr(self, "probe_t"):
            self.probe_t = self.dbg("probe", [128, 18 * W], F32)
            self.probe_i = {}

        def prb(label, ap, c_, h_, width=W):
            if not (probe_on and c_ == 0 and sig == 0 and h_ == 0):
                return
            idx = self.probe_i.setdefault(label, len(self.probe_i))
            tok = self.sV.inc(v.nop())
            self.wait(sy, tok)
            self.wait(sy, self.sA.tok())
            dtok = self.dS.inc(sy.dma_start(
                out=self.probe_t[:, idx * W:idx * W + width], in_=ap,
                allow_slow_non_contiguous=True), 16)
            self.wait(v, dtok)
            self.wait(a, dtok)
        NCHK = W // L
        self.wait(sy, self.p0_done)
        self.wait(v, self.dO.tok())
        for kk in range(NCH):
            for ftok in (self.xnfm_free or []):
                self.wait(sy, ftok)
            tok = self.dT.inc(sy.dma_start_transpose(
                out=self.xnfm[kk][:],
                in_=self.xn_scr[sig * S:(sig + 1) * S, kk * 128:(kk + 1) * 128]), 16)
        xn_tok = tok

        wtoks = [None] * NCH

        def prefetch_w(c):
            wb = self.wsl[c % 2]
            srcs = [
                (self.Wq, c * 128), (self.Wk, c * 128), (self.Wv, c * 128),
                (self.Wa, c * 128), (self.Wa, D + c * 128),
            ]
            self.wait(gp, self.wsl_free[c % 2])
            tok2 = None
            for t_i, (wsrc, coff) in enumerate(srcs):
                for kk in range(NCH):
                    tok2 = self.dW.inc(gp.dma_start(
                        out=wb[t_i][:, kk * 128:(kk + 1) * 128],
                        in_=wsrc[kk * 128:(kk + 1) * 128, coff:coff + 128]), 16)
            wtoks[c] = tok2

        prefetch_w(0)
        stats = self.gnstats[sig]
        self.V("memset", stats[:], 0.0)

        for c in range(NCH):
            if c + 1 < NCH:
                prefetch_w(c + 1)
            wb = self.wsl[c % 2]
            self.V("memset", self.hseed[:], 0.0, dr=False)
            seed_tok = self.sV.inc(v.nop())
            last_mm_tok = None
            for h in range(2):
                base = h * W
                ev_tok = {}
                for ns in range(2):
                    mm_tok = {}
                    for t_i in range(5):
                        slot = self.pb[t_i]
                        self.wait(te, wtoks[c])
                        self.wait(te, xn_tok)
                        self.wait(te, self.slot_free[t_i])
                        for kk in range(NCH):
                            ins = te.matmul(
                                slot[:, :],
                                wb[t_i][:, kk * 128:(kk + 1) * 128],
                                self.xnfm[kk][:, base + ns * 512:base + (ns + 1) * 512],
                                start=(kk == 0), stop=(kk == NCH - 1))
                        mm_tok[t_i] = self.sT.inc(ins)
                    cs = slice(ns * 512, (ns + 1) * 512)
                    self.wait(a, mm_tok[0])
                    if getattr(self, "lnout_tok", None):
                        self.wait(a, self.lnout_tok[0])
                        self.wait(a, self.lnout_tok[1])
                    e0 = self.sA.inc(self.A("copy", out=self.q_s[:, cs], in_=self.pb[0][:, :]))
                    self.wait(a, mm_tok[1])
                    e1 = self.sA.inc(self.A("copy", out=self.dump_bf[:, cs], in_=self.pb[1][:, :]))
                    self.wait(v, mm_tok[2])
                    self.wait(v, e1)
                    e2 = self.sV.inc(self.V("tensor_tensor", out=P[2][:, cs],
                                            in0=self.dump_bf[:, cs], in1=self.pb[2][:, :], op=OP.mult))
                    self.wait(v, mm_tok[3])
                    e3 = self.sV.inc(self.V("tensor_copy", out=P[0][:, cs], in_=self.pb[3][:, :]))
                    self.wait(v, mm_tok[4])
                    e4 = self.sV.inc(self.V("tensor_copy", out=P[1][:, cs], in_=self.pb[4][:, :]))
                    self.slot_free = [e0, e1, e2, e3, e4] + self.slot_free[5:]
                evac_tok_v = self.sV.tok()
                evac_tok_a = self.sA.tok()
                prb("ar", P[0][:], c, h); prb("ai", P[1][:], c, h); prb("kv", P[2][:], c, h)

                self.wait(a, evac_tok_v)
                t1 = self.sA.inc(self.A("activation", out=P[3][:], in_=P[0][:], func=AF.Square))
                t2 = self.sA.inc(self.A("activation", out=P[4][:], in_=P[1][:], func=AF.Square))
                self.wait(v, t2)
                m2 = self.sV.inc(self.V("tensor_tensor", out=P[3][:], in0=P[3][:], in1=P[4][:], op=OP.add))
                self.wait(a, m2)
                ma = self.sA.inc(self.A("activation", out=P[5][:], in_=P[3][:], func=AF.Sqrt))
                aai = self.sA.inc(self.A("activation", out=P[4][:], in_=P[1][:], func=AF.Abs))
                sgn = self.sA.inc(self.A("activation", out=P[1][:], in_=P[1][:], func=AF.Sign))
                sg_ = self.A("activation", out=P[6][:], in_=P[5][:], func=AF.Sigmoid)
                lm = self.sA.inc(self.A("activation", out=P[6][:], in_=P[6][:], func=AF.Ln))
                self.wait(v, aai)
                den = self.sV.inc(self.V("tensor_tensor", out=P[4][:], in0=P[5][:], in1=P[4][:], op=OP.add))
                self.wait(a, den)
                self.A("activation", out=P[5][:], in_=P[4][:], func=AF.Ln, bias=self.SC("tiny"))
                rec = self.sA.inc(self.A("activation", out=P[5][:], in_=P[5][:], func=AF.Exp,
                                         scale=self.SC("neg_one")))
                self.wait(v, rec)
                rr = self.sV.inc(self.V("tensor_tensor", out=P[0][:], in0=P[0][:], in1=P[5][:], op=OP.mult))
                self.wait(a, rr)
                tq0 = self.sA.inc(self.A("activation", out=P[0][:], in_=P[0][:], func=AF.Arctan))
                self.wait(v, tq0)
                self.V("tensor_scalar", out=P[0][:], in0=P[0][:], scalar1=-0.5,
                       scalar2=float(math.pi / 8.0), op0=OP.mult, op1=OP.add)
                tq = self.sV.inc(self.V("tensor_tensor", out=P[1][:], in0=P[0][:], in1=P[1][:], op=OP.mult))
                prb("lm", P[6][:], c, h)
                prb("tq", P[1][:], c, h)
                self.wait(v, lm)
                self.V("tensor_tensor_scan", out=P[6][:], data0=self.c_cmask[:], data1=P[6][:],
                       initial=0.0, op0=OP.mult, op1=OP.add)
                lmc_tc = self.sV.inc(
                    self.V("tensor_tensor_scan", out=P[1][:], data0=self.c_cmask[:], data1=P[1][:],
                           initial=0.0, op0=OP.mult, op1=OP.add))
                prb("lmc", P[6][:], c, h)
                prb("tc", P[1][:], c, h)
                self.wait(a, lmc_tc)
                self.A("activation", out=P[3][:], in_=P[6][:], func=AF.Exp)
                GGi = self.sA.inc(self.A("activation", out=P[6][:], in_=P[6][:], func=AF.Exp,
                                         scale=self.SC("neg_one")))
                self.V("tensor_scalar_mul", out=P[5][:], in0=P[1][:], scalar1=4.0)
                self.V("tensor_scalar", out=P[7][:], in0=P[1][:], scalar1=float(2.0 / math.pi),
                       scalar2=MAGIC, op0=OP.mult, op1=OP.add)
                self.V("tensor_scalar_sub", out=P[7][:], in0=P[7][:], scalar1=MAGIC)
                self.V("tensor_scalar_mul", out=P[7][:], in0=P[7][:], scalar1=TWO_PI)
                self.V("tensor_tensor", out=P[7][:], in0=P[5][:], in1=P[7][:], op=OP.subtract)
                self.V("tensor_scalar_min", out=P[7][:], in0=P[7][:], scalar1=PI_CLAMP)
                th = self.sV.inc(self.V("tensor_scalar_max", out=P[7][:], in0=P[7][:], scalar1=-PI_CLAMP))
                self.wait(a, th)
                sinr = self.sA.inc(self.A("activation", out=P[7][:], in_=P[7][:], func=AF.Sin))
                self.V("tensor_scalar", out=P[8][:], in0=P[1][:], scalar1=float(2.0 / math.pi),
                       scalar2=0.25, op0=OP.mult, op1=OP.add)
                self.V("tensor_scalar_add", out=P[8][:], in0=P[8][:], scalar1=MAGIC)
                self.V("tensor_scalar_sub", out=P[8][:], in0=P[8][:], scalar1=MAGIC)
                self.V("tensor_scalar", out=P[8][:], in0=P[8][:], scalar1=TWO_PI,
                       scalar2=float(math.pi / 2.0), op0=OP.mult, op1=OP.subtract)
                self.V("tensor_tensor", out=P[8][:], in0=P[5][:], in1=P[8][:], op=OP.subtract)
                self.V("tensor_scalar_min", out=P[8][:], in0=P[8][:], scalar1=PI_CLAMP)
                th2 = self.sV.inc(self.V("tensor_scalar_max", out=P[8][:], in0=P[8][:], scalar1=-PI_CLAMP))
                self.wait(a, th2)
                cosr = self.sA.inc(self.A("activation", out=P[8][:], in_=P[8][:], func=AF.Sin))
                prb("G", P[3][:], c, h); prb("Gi", P[6][:], c, h)
                prb("sinr", P[7][:], c, h); prb("cosr", P[8][:], c, h)
                self.wait(v, GGi)
                self.V("tensor_tensor", out=P[2][:], in0=P[2][:], in1=P[6][:], op=OP.mult)
                self.wait(v, cosr)
                self.V("tensor_tensor", out=P[5][:], in0=P[2][:], in1=P[8][:], op=OP.mult)
                self.V("tensor_tensor", out=P[2][:], in0=P[2][:], in1=P[7][:], op=OP.mult)
                self.V("tensor_tensor_scan", out=P[5][:], data0=self.c_cmask[:], data1=P[5][:],
                       initial=0.0, op0=OP.mult, op1=OP.add)
                zci = self.sV.inc(
                    self.V("tensor_tensor_scan", out=P[2][:], data0=self.c_cmask[:], data1=P[2][:],
                           initial=0.0, op0=OP.mult, op1=OP.add))
                prb("Zcr", P[5][:], c, h); prb("Zci", P[2][:], c, h)
                self.V("tensor_tensor", out=self.pend[:, 0:NCHK], in0=self._ends(P[3]),
                       in1=self._ends(P[8]), op=OP.mult, dr=False)
                self.V("tensor_tensor", out=self.pend[:, 8:8 + NCHK], in0=self._ends(P[3]),
                       in1=self._ends(P[7]), op=OP.mult)
                if h == 0:
                    self.wait(v, seed_tok)
                self.V("tensor_copy", out=self.hinr[:, 0:1], in_=self.hseed[:, 0:1], dr=False)
                self.V("tensor_copy", out=self.hini[:, 0:1], in_=self.hseed[:, 1:2])
                for j in range(NCHK):
                    zre = P[5][:, j * L + L - 1:j * L + L]
                    zie = P[2][:, j * L + L - 1:j * L + L]
                    prj = self.pend[:, j:j + 1]
                    pij = self.pend[:, 8 + j:9 + j]
                    orow = (self.hinr[:, j + 1:j + 2], self.hini[:, j + 1:j + 2]) \
                        if j + 1 < NCHK + 1 and j != NCHK - 1 else \
                        (self.hseed[:, 0:1], self.hseed[:, 1:2])
                    self.V("scalar_tensor_tensor", out=self.gntmp[:, 4:5], in0=self.hinr[:, j:j + 1],
                           scalar=zre, in1=prj, op0=OP.add, op1=OP.mult, dr=False)
                    self.V("scalar_tensor_tensor", out=self.gntmp[:, 5:6], in0=self.hini[:, j:j + 1],
                           scalar=zie, in1=pij, op0=OP.subtract, op1=OP.mult)
                    self.V("tensor_tensor", out=orow[0], in0=self.gntmp[:, 4:5],
                           in1=self.gntmp[:, 5:6], op=OP.subtract, dr=False)
                    self.V("scalar_tensor_tensor", out=self.gntmp[:, 4:5], in0=self.hini[:, j:j + 1],
                           scalar=zie, in1=prj, op0=OP.subtract, op1=OP.mult, dr=False)
                    self.V("scalar_tensor_tensor", out=self.gntmp[:, 5:6], in0=self.hinr[:, j:j + 1],
                           scalar=zre, in1=pij, op0=OP.add, op1=OP.mult)
                    self.V("tensor_tensor", out=orow[1], in0=self.gntmp[:, 4:5],
                           in1=self.gntmp[:, 5:6], op=OP.add)
                prb("hinr", self.hinr[:, 0:8], c, h, width=8)
                prb("hini", self.hini[:, 0:8], c, h, width=8)
                self.V("tensor_tensor", out=self._c3(P[5]), in0=self._c3(P[5]),
                       in1=self._bc(self.hinr, NCHK), op=OP.add)
                self.V("tensor_tensor", out=self._c3(P[2]), in0=self._bc(self.hini, NCHK),
                       in1=self._c3(P[2]), op=OP.subtract)
                self.V("tensor_tensor", out=P[4][:], in0=P[8][:], in1=P[5][:], op=OP.mult, dr=False)
                self.V("tensor_tensor", out=P[6][:], in0=P[7][:], in1=P[2][:], op=OP.mult)
                self.V("tensor_tensor", out=P[4][:], in0=P[4][:], in1=P[6][:], op=OP.subtract)
                self.V("tensor_tensor", out=P[8][:], in0=P[8][:], in1=P[2][:], op=OP.mult, dr=False)
                self.V("tensor_tensor", out=P[7][:], in0=P[7][:], in1=P[5][:], op=OP.mult)
                self.V("tensor_tensor", out=P[8][:], in0=P[8][:], in1=P[7][:], op=OP.add)
                self.wait(v, evac_tok_a)
                self.V("tensor_tensor", out=P[6][:], in0=self.q_s[:], in1=P[3][:], op=OP.mult)
                yslice = self.yr_all[c][:, base:base + W]
                yrtok = self.sV.inc(self.V("tensor_tensor", out=yslice, in0=P[6][:], in1=P[4][:],
                                           op=OP.mult, dr=False))
                yitok = self.sV.inc(self.V("tensor_tensor", out=self.yi_buf[:], in0=P[6][:],
                                           in1=P[8][:], op=OP.mult))
                self.wait(a, yitok)
                sq1 = self.sA.inc(self.A("activation", out=self.dump_bf[:], in_=yslice,
                                         func=AF.Square, accum_out=self.gntmp[:, 0:1]))
                sq2 = self.sA.inc(self.A("activation", out=self.dump_bf[:], in_=self.yi_buf[:],
                                         func=AF.Square, accum_out=self.gntmp[:, 1:2]))
                self.V("tensor_reduce", out=self.gntmp[:, 2:3], in_=yslice,
                       axis=mybir.AxisListType.X, op=OP.add)
                self.V("tensor_reduce", out=self.gntmp[:, 3:4], in_=self.yi_buf[:],
                       axis=mybir.AxisListType.X, op=OP.add)
                sc0 = stats[:, 3 * c:3 * c + 1]
                sc1 = stats[:, 3 * c + 1:3 * c + 2]
                sc2 = stats[:, 3 * c + 2:3 * c + 3]
                self.V("tensor_tensor", out=sc0, in0=sc0, in1=self.gntmp[:, 2:3], op=OP.add, dr=False)
                self.V("tensor_tensor", out=sc1, in0=sc1, in1=self.gntmp[:, 3:4], op=OP.add)
                self.wait(v, sq2)
                self.V("tensor_tensor", out=self.gntmp[:, 4:5], in0=self.gntmp[:, 0:1],
                       in1=self.gntmp[:, 1:2], op=OP.add)
                last_mm_tok = self.sV.inc(
                    self.V("tensor_tensor", out=sc2, in0=sc2, in1=self.gntmp[:, 4:5], op=OP.add))
            self.wsl_free[c % 2] = self.sT.tok()
        self.p1_done = [self.sV.tok(), self.sA.tok()]
        self.xnfm_free = [self.sT.tok()]

    def p2_router(self, sig):
        nc = self.nc
        v, a, te, sy, gp = nc.vector, nc.scalar, nc.tensor, nc.sync, nc.gpsimd
        if not hasattr(self, "_rw_done"):
            self._rw_done = True
            rtok = self.dW.inc(gp.dma_start(out=self.rw_sb[:], in_=self.rw), 16)
            self.wait(te, rtok)
            self.wait(te, self.const_tok)
            for kk in range(NCH):
                self.wait(te, self.slot_free[7])
                tt = self.sT.inc(te.transpose(self.pt[:, 0:E], self.rw_sb[0:E, kk * 128:(kk + 1) * 128],
                                             self.c_identb[0:E, 0:E]))
                self.wait(v, tt)
                self.slot_free[7] = self.sV.inc(self.V("tensor_copy", out=self.rwt[kk][:],
                                                       in_=self.pt[:, 0:E], dr=False))
            self._rwt_ready = self.sV.tok()
        for tau in range(S // 128):
            self.wait(te, self._rwt_ready)
            self.wait(te, self.slot_free[5])
            for kk in range(NCH):
                ins = te.matmul(self.pb[5][:, 0:E], self.xnfm[kk][:, tau * 128:(tau + 1) * 128],
                                self.rwt[kk][:], start=(kk == 0), stop=(kk == NCH - 1))
            mm = self.sT.inc(ins)
            st = self.r_st
            self.wait(v, mm)
            self.V("tensor_reduce", out=st[:, 0:1], in_=self.pb[5][:, 0:E],
                   axis=mybir.AxisListType.X, op=OP.max)
            self.V("tensor_scalar_mul", out=st[:, 1:2], in0=st[:, 0:1], scalar1=-1.0)
            ex = self.sV.inc(v.nop())
            self.wait(a, ex)
            ea = self.sA.inc(self.A("activation", out=self.r_e[:], in_=self.pb[5][:, 0:E],
                                    func=AF.Exp, bias=st[:, 1:2], accum_out=st[:, 2:3]))
            self.slot_free[5] = ea
            self.wait(v, ea)
            self.V("tensor_reduce", out=st[:, 3:4], in_=self.r_e[:], axis=mybir.AxisListType.X,
                   op=OP.max)
            self.V("tensor_single_scalar", out=self.r_eq[:], in_=self.r_e[:], scalar=st[:, 3:4],
                   op=OP.is_equal)
            self.wait(v, getattr(self, "_rs_tt", None))
            self.V("tensor_tensor", out=self.r_s[:], in0=self.r_eq[:], in1=self.r_e[:], op=OP.mult)
            self.V("tensor_tensor", out=self.r_s[:], in0=self.r_e[:], in1=self.r_s[:], op=OP.subtract)
            self.V("tensor_reduce", out=st[:, 4:5], in_=self.r_s[:], axis=mybir.AxisListType.X,
                   op=OP.max)
            self.V("tensor_tensor", out=st[:, 5:6], in0=st[:, 3:4], in1=st[:, 4:5], op=OP.add)
            self.V("scalar_tensor_tensor", out=st[:, 5:6], in0=st[:, 2:3], scalar=1e-6,
                   in1=st[:, 5:6], op0=OP.mult, op1=OP.add)
            self.V("reciprocal", out=st[:, 6:7], in_=st[:, 5:6])
            self.V("tensor_tensor", out=st[:, 0:1], in0=st[:, 3:4], in1=st[:, 6:7], op=OP.mult)
            self.V("tensor_tensor", out=st[:, 1:2], in0=st[:, 4:5], in1=st[:, 6:7], op=OP.mult)
            self.V("tensor_single_scalar", out=self.r_s[:], in_=self.r_s[:], scalar=st[:, 4:5],
                   op=OP.is_equal)
            self.V("tensor_scalar_mul", out=self.r_s[:], in0=self.r_s[:], scalar1=st[:, 1:2])
            self.V("tensor_scalar_mul", out=self.r_eq[:], in0=self.r_eq[:], scalar1=st[:, 0:1])
            sdone = self.sV.inc(self.V("tensor_tensor", out=self.r_s[:], in0=self.r_s[:],
                                       in1=self.r_eq[:], op=OP.add))
            self.wait(te, sdone)
            self.wait(te, self.slot_free[6])
            tt = self.sT.inc(te.transpose(self.pb[6][0:E, 0:128], self.r_s[:], self.c_identf[:]))
            self._rs_tt = tt
            self.wait(v, tt)
            self.wait(v, getattr(self, "_rst_store", None))
            cpd = self.sV.inc(self.V("tensor_copy", out=self.r_sT[:], in_=self.pb[6][0:E, 0:128],
                                     dr=False))
            self.slot_free[6] = cpd
            self.wait(sy, cpd)
            self._rst_store = self.dS.inc(
                sy.dma_start(out=self.s_scr[:, sig * S + tau * 128:sig * S + (tau + 1) * 128],
                             in_=self.r_sT[:]), 16)

    def p3_gn_wo_ln1(self, sig):
        nc = self.nc
        v, a, te, sy, gp = nc.vector, nc.scalar, nc.tensor, nc.sync, nc.gpsimd
        if not hasattr(self, "wo_sl"):
            self.wo_sl = [self._ssb(f"wosl{k}", [128, D], BF16) for k in range(NCH)]
            self.wg_sl = [self._ssb(f"wgsl{k}", [128, D], BF16) for k in range(NCH)]
            wtok = None
            for kk in range(NCH):
                self.dW.inc(gp.dma_start(out=self.wo_sl[kk][:],
                                         in_=self.Wo[kk * 128:(kk + 1) * 128, :]), 16)
                wtok = self.dW.inc(gp.dma_start(out=self.wg_sl[kk][:],
                                                in_=self.Wg[kk * 128:(kk + 1) * 128, :]), 16)
            self.wog_tok = wtok
            self.grp = self._ssb("grp", [4, 3 * NCH], F32)
            self.gpp = self._ssb("gpp", [128, 2 * NCH], F32)
            self.gab = self._ssb("gab", [128, 2 * NCH], F32)
            self.sg_buf = self._ssb("sg_buf", [128, S], BF16)
            self.z_all = [self._ssb(f"zal{c}", [128, S], BF16) for c in range(NCH)]
        stats = self.gnstats[sig]
        self.wait(v, self.p1_done[1])
        for c in range(NCH):
            self.wait(te, self.p1_done[0])
            self.wait(te, self.const_tok)
            self.wait(te, self.slot_free[5])
            mm = self.sT.inc(te.matmul(self.pb[5][0:4, 0:3], self.c_g4ind[:],
                                       stats[:, 3 * c:3 * c + 3], start=True, stop=True))
            self.wait(v, mm)
            self.slot_free[5] = self.sV.inc(
                self.V("tensor_copy", out=self.grp[:, 3 * c:3 * c + 3], in_=self.pb[5][0:4, 0:3]))
        NG = float(S * DG)
        g3 = self.grp[:].rearrange("p (c q) -> p c q", q=3)
        mrs = g3[:, :, 0:1]; mis = g3[:, :, 1:2]; m2s = g3[:, :, 2:3]
        self.V("tensor_scalar_mul", out=mrs, in0=mrs, scalar1=1.0 / NG)
        self.V("tensor_scalar_mul", out=mis, in0=mis, scalar1=1.0 / NG)
        self.V("tensor_scalar_mul", out=m2s, in0=m2s, scalar1=1.0 / NG)
        self.V("tensor_tensor", out=self.pend[0:4, 0:NCH], in0=mrs, in1=mrs, op=OP.mult, dr=False)
        self.V("tensor_tensor", out=self.pend[0:4, 8:8 + NCH], in0=mis, in1=mis, op=OP.mult)
        self.V("tensor_tensor", out=m2s,
               in0=m2s, in1=self.pend[0:4, 0:NCH].rearrange("p (c q) -> p c q", q=1), op=OP.subtract)
        vtok = self.sV.inc(
            self.V("tensor_tensor", out=m2s,
                   in0=m2s, in1=self.pend[0:4, 8:8 + NCH].rearrange("p (c q) -> p c q", q=1),
                   op=OP.subtract))
        self.wait(a, vtok)
        self.A("activation", out=m2s, in_=m2s, func=AF.Ln, bias=self.c_sconst[0:4, 1:2])
        rsv = self.sA.inc(self.A("activation", out=m2s, in_=m2s, func=AF.Exp,
                                 scale=self.c_sconst[0:4, 2:3]))
        for c in range(NCH):
            self.wait(te, rsv)
            self.wait(te, self.slot_free[5])
            mm = self.sT.inc(te.matmul(self.pb[5][:, 0:2], self.c_g4exp[:],
                                       self.grp[:, 3 * c:3 * c + 3:2], start=True, stop=True))
            self.wait(v, mm)
            self.slot_free[5] = self.sV.inc(
                self.V("tensor_copy", out=self.gpp[:, 2 * c:2 * c + 2], in_=self.pb[5][:, 0:2]))
            self.V("tensor_tensor", out=self.gab[:, 2 * c:2 * c + 1],
                   in0=self.gpp[:, 2 * c + 1:2 * c + 2], in1=self.p_gns[:, c:c + 1], op=OP.mult)
            self.V("tensor_tensor", out=self.gntmp[:, 0:1], in0=self.gpp[:, 2 * c:2 * c + 1],
                   in1=self.gab[:, 2 * c:2 * c + 1], op=OP.mult)
            self.V("tensor_tensor", out=self.gab[:, 2 * c + 1:2 * c + 2],
                   in0=self.p_gnb[:, c:c + 1], in1=self.gntmp[:, 0:1], op=OP.subtract)
        gn_tok = self.sV.tok()
        self.z_prev_tok = None
        for c in range(NCH):
            for ns in range(4):
                self.wait(te, self.wog_tok)
                self.wait(te, self.slot_free[ns % 2])
                for kk in range(NCH):
                    ins = te.matmul(self.pb[ns % 2][:, :],
                                    self.wg_sl[kk][:, c * 128:(c + 1) * 128],
                                    self.xnfm[kk][:, ns * 512:(ns + 1) * 512],
                                    start=(kk == 0), stop=(kk == NCH - 1))
                mm = self.sT.inc(ins)
                self.wait(a, mm)
                if ns == 0:
                    self.wait(a, self.z_prev_tok)
                self.slot_free[ns % 2] = self.sA.inc(
                    self.A("activation", out=self.sg_buf[:, ns * 512:(ns + 1) * 512],
                           in_=self.pb[ns % 2][:, :], func=AF.Silu))
            sg_tok = self.sA.tok()
            self.wait(a, gn_tok)
            if "yrp3" in self.debug and c == self.PROBE_C and sig == 0:
                yt_ = self.dbg("yrp3", [128, S], BF16)
                self.wait(sy, self.sV.tok())
                ydt = self.dO.inc(sy.dma_start(out=yt_, in_=self.yr_all[self.PROBE_C][:]), 16)
                self.wait(v, ydt)
            if "sg" in self.debug and c == self.PROBE_C and sig == 0:
                sgt = self.dbg("sg", [128, S], BF16)
                self.wait(sy, self.sA.tok())
                sgd = self.dO.inc(sy.dma_start(out=sgt, in_=self.sg_buf[:]), 16)
            if "gab2" in self.debug and c == 0 and sig == 0:
                g2 = self.dbg("gab2", [128, 2 * NCH])
                self.wait(sy, self.sV.tok())
                g2t = self.dO.inc(sy.dma_start(out=g2, in_=self.gab[:]), 16)
                self.wait(v, g2t)
            self.wait(v, gn_tok)
            self.wait(v, self.sA.tok())
            for hh in range(2):
                hs = slice(hh * W, (hh + 1) * W)
                yn1 = self.sV.inc(
                    self.V("tensor_scalar", out=self.pool[2][:], in0=self.yr_all[c][:, hs],
                           scalar1=self.gab[:, 2 * c:2 * c + 1],
                           scalar2=self.gab[:, 2 * c + 1:2 * c + 2], op0=OP.mult, op1=OP.add))
                if "yn" in self.debug and c == self.PROBE_C and sig == 0 and hh == 0:
                    ynt = self.dbg("yn", [128, W])
                    self.wait(sy, yn1)
                    yntok = self.dO.inc(sy.dma_start(out=ynt, in_=self.pool[2][:]), 16)
                    self.wait(v, yntok)
                ztok = self.sV.inc(self.V("tensor_tensor", out=self.z_all[c][:, hs],
                                          in0=self.pool[2][:], in1=self.sg_buf[:, hs], op=OP.mult))
            self.z_prev_tok = ztok
        z_tok = self.sV.tok()
        x1_all = self.yr_all
        for oc in range(NCH):
            for ns in range(4):
                self.wait(te, z_tok)
                self.wait(te, self.slot_free[2 + (ns % 2)])
                for kk in range(NCH):
                    ins = te.matmul(self.pb[2 + (ns % 2)][:, :],
                                    self.wo_sl[kk][:, oc * 128:(oc + 1) * 128],
                                    self.z_all[kk][:, ns * 512:(ns + 1) * 512],
                                    start=(kk == 0), stop=(kk == NCH - 1))
                mm = self.sT.inc(ins)
                self.wait(v, mm)
                self.slot_free[2 + (ns % 2)] = self.sV.inc(
                    self.V("tensor_tensor", out=x1_all[oc][:, ns * 512:(ns + 1) * 512],
                           in0=self.xnfm[oc][:, ns * 512:(ns + 1) * 512],
                           in1=self.pb[2 + (ns % 2)][:, :], op=OP.add))
        x1_tok = self.sV.tok()
        if "z" in self.debug:
            zt = self.dbg("z", [D, T], BF16)
            self.wait(self.nc.sync, z_tok)
            for c in range(NCH):
                self.dO.inc(self.nc.sync.dma_start(
                    out=zt[c * 128:(c + 1) * 128, sig * S:(sig + 1) * S], in_=self.z_all[c][:]), 16)
            self.wait(self.nc.vector, self.dO.tok())
        if "x1" in self.debug:
            xt_ = self.dbg("x1", [D, T], BF16)
            self.wait(self.nc.sync, x1_tok)
            for c in range(NCH):
                self.dO.inc(self.nc.sync.dma_start(
                    out=xt_[c * 128:(c + 1) * 128, sig * S:(sig + 1) * S], in_=x1_all[c][:]), 16)
            self.wait(self.nc.vector, self.dO.tok())
        if "gab" in self.debug:
            gt = self.dbg("gab", [128, 2 * NCH])
            self.wait(self.nc.sync, gn_tok)
            self.dO.inc(self.nc.sync.dma_start(out=gt, in_=self.gab[:]), 16)
        self.dbg("t", [D, T], BF16)
        self.ln_scr = (self.pool[0][:, 0:512], self.pool[1][:, 0:512], self.dump_bf[:, 0:512],
                       self.pool[2][0:1, 0:512], self.pool[2][0:1, 512:1024],
                       self.pool[3][0:1, 0:512],
                       self.q_s[:, 0:512], self.q_s[:, 512:1024])
        self.lnout_tok = [None, None]
        self.ln_fm(x1_all, self.p_ln1s, self.p_ln1b, EPS_IN, x1_tok,
                   store=self.t_scr, sig=sig, dbg_name="t")
        self.xnfm_free = [self.sT.tok(), self.sV.tok()]

    def ln_fm(self, src_tiles, gamma, beta, eps, src_tok, store, sig, dbg_name=None):
        nc = self.nc
        v, a, te, sy = nc.vector, nc.scalar, nc.tensor, nc.sync
        width = src_tiles[0].shape[1]
        nns = width // 512
        ln_p0, ln_p1, ln_sq, r_mean, r_var, r_msq, lo0, lo1 = self.ln_scr
        lnout = [lo0, lo1]
        self.wait(te, src_tok)
        self.wait(self.nc.scalar, src_tok)
        self.wait(te, self.const_tok)
        dbg = self.dbg_tensors.get(dbg_name) if dbg_name else None
        sq_tok = None
        for ns in range(nns):
            cs = slice(ns * 512, (ns + 1) * 512)
            self.wait(te, self.slot_free[3])
            self.wait(te, self.slot_free[4])
            for oc in range(NCH):
                ins = te.matmul(self.pb[3][0:1, :], self.c_onescol[:, 0:1],
                                src_tiles[oc][:, cs], start=(oc == 0), stop=(oc == NCH - 1))
                self.wait(a, sq_tok)
                sq = self.sA.inc(self.A("activation", out=ln_sq,
                                        in_=src_tiles[oc][:, cs], func=AF.Square))
                self.wait(te, sq)
                ins2 = te.matmul(self.pb[4][0:1, :], self.c_onescol[:, 0:1],
                                 ln_sq, start=(oc == 0), stop=(oc == NCH - 1))
                sq_tok = self.sT.inc(ins2)
            mm = sq_tok
            self.wait(v, mm)
            self.V("tensor_copy", out=r_mean, in_=self.pb[3][0:1, :], dr=False)
            ev = self.sV.inc(self.V("tensor_copy", out=r_var, in_=self.pb[4][0:1, :]))
            self.V("tensor_scalar_mul", out=r_mean, in0=r_mean, scalar1=1.0 / D)
            self.V("tensor_tensor", out=r_msq, in0=r_mean, in1=r_mean, op=OP.mult)
            rv = self.sV.inc(
                self.V("scalar_tensor_tensor", out=r_var, in0=r_var,
                       scalar=1.0 / D, in1=r_msq, op0=OP.mult, op1=OP.subtract))
            self.wait(a, rv)
            ek = "eps_in" if eps == EPS_IN else "eps_out"
            self.A("activation", out=r_var, in_=r_var, func=AF.Ln,
                   bias=self.c_sconst[0:1, self.SCN[ek]:self.SCN[ek] + 1])
            rsv = self.sA.inc(self.A("activation", out=r_var, in_=r_var,
                                     func=AF.Exp, scale=self.c_sconst[0:1, 2:3]))
            self.slot_free[3] = ev
            self.slot_free[4] = ev
            if "rows" in self.debug and sig == 0 and store is self.t_scr:
                rt = self.dbg("rows", [8, 1536])
                self.wait(sy, rsv)
                rdt = self.dO.inc(sy.dma_start(out=rt[ns:ns + 1, :], in_=rows[0:1, :]), 16)
                self.wait(v, rdt)
                self.wait(self.nc.scalar, rdt)
            self.wait(te, ev)
            self.wait(te, rsv)
            mb = self.sT.inc(te.matmul(self.pb[3][:, :], self.c_onesrow[:], r_mean,
                                       start=True, stop=True))
            rb = self.sT.inc(te.matmul(self.pb[4][:, :], self.c_onesrow[:], r_var,
                                       start=True, stop=True))
            self.wait(v, rb)
            for oc in range(NCH):
                self.V("tensor_tensor", out=ln_p0, in0=src_tiles[oc][:, cs],
                       in1=self.pb[3][:, :], op=OP.subtract)
                self.V("tensor_tensor", out=ln_p1, in0=ln_p0,
                       in1=self.pb[4][:, :], op=OP.mult)
                self.wait(v, self.lnout_tok[oc % 2])
                stok = self.sV.inc(
                    self.V("tensor_scalar", out=lnout[oc % 2], in0=ln_p1,
                           scalar1=gamma[:, oc:oc + 1], scalar2=beta[:, oc:oc + 1],
                           op0=OP.mult, op1=OP.add))
                self.wait(sy, stok)
                col0 = sig * width + ns * 512
                dtk = self.dS.inc(sy.dma_start(
                    out=store[oc * 128:(oc + 1) * 128, col0:col0 + 512],
                    in_=lnout[oc % 2]), 16)
                if dbg is not None:
                    dtk = self.dO.inc(sy.dma_start(
                        out=dbg[oc * 128:(oc + 1) * 128, col0:col0 + 512],
                        in_=lnout[oc % 2]), 16)
                self.lnout_tok[oc % 2] = dtk
            self.slot_free[3] = self.sV.tok()
            self.slot_free[4] = self.sV.tok()

    def phase_barrier(self):
        nc = self.nc
        toks = [self.sT.tok(), self.sV.tok(), self.sA.tok()]
        for eng in (nc.sync, nc.gpsimd, nc.tensor, nc.vector, nc.scalar):
            for tk in toks:
                self.wait(eng, tk)

    def p4_ffn(self):
        nc = self.nc
        v, a, te, sy, gp = nc.vector, nc.scalar, nc.tensor, nc.sync, nc.gpsimd
        self.phase_barrier()
        st4 = contextlib.ExitStack()
        sb4 = lambda n, s, t: st4.enter_context(nc.sbuf_tensor(n, s, t))
        TG = FFN_TG
        t_sl = [sb4(f"p4t{k}", [128, TG], BF16) for k in range(NCH)]
        h1 = [sb4(f"p4h{j}", [128, TG], BF16) for j in range(F // 128)]
        w1b = [sb4(f"p4w1_{b}", [128, 8 * 512], BF16) for b in range(2)]
        w2b = [sb4(f"p4w2_{k}", [128, D], BF16) for k in range(F // 128)]
        s2 = [sb4(f"p4s2{k}", [128, TG], BF16) for k in range(NCH)]
        w1_free = [None, None]
        self.t_free = None
        self.w2_free = None
        lnp0 = sb4("p4lnp0", [128, 512], F32)
        lnp1 = sb4("p4lnp1", [128, 512], F32)
        lnsq = sb4("p4lnsq", [128, 512], BF16)
        lnrows = sb4("p4lnrows", [1, 1536], F32)
        lnoutb = [sb4(f"p4lo{i}", [128, 512], BF16) for i in range(2)]
        self.ln_scr = (lnp0[:], lnp1[:], lnsq[:], lnrows[0:1, 0:512], lnrows[0:1, 512:1024],
                       lnrows[0:1, 1024:1536], lnoutb[0][:], lnoutb[1][:])
        self.lnout_tok = [None, None]
        for tg in range(T // TG):
            ttok = None
            for kk in range(NCH):
                self.wait(sy, self.t_free)
                ttok = self.dS.inc(sy.dma_start(
                    out=t_sl[kk][:], in_=self.t_scr[kk * 128:(kk + 1) * 128, tg * TG:(tg + 1) * TG]), 16)
            w2tok = None
            for kk2 in range(F // 128):
                self.wait(gp, self.w2_free)
                w2tok = self.dW.inc(gp.dma_start(
                    out=w2b[kk2][:], in_=self.W2[kk2 * 128:(kk2 + 1) * 128, :]), 16)
            for jg in range(8):
                wtok = None
                self.wait(gp, w1_free[jg % 2])
                for kk in range(NCH):
                    wtok = self.dW.inc(gp.dma_start(
                        out=w1b[jg % 2][:, kk * 512:(kk + 1) * 512],
                        in_=self.W1[kk * 128:(kk + 1) * 128, jg * 512:(jg + 1) * 512]), 16)
                for j4 in range(4):
                    j = jg * 4 + j4
                    for ns in range(TG // 512):
                        slot = (j4 * 2 + ns) % 2
                        self.wait(te, wtok)
                        self.wait(te, ttok)
                        self.wait(te, self.slot_free[slot])
                        for kk in range(NCH):
                            ins = te.matmul(self.pb[slot][:, :],
                                            w1b[jg % 2][:, kk * 512 + j4 * 128:kk * 512 + (j4 + 1) * 128],
                                            t_sl[kk][:, ns * 512:(ns + 1) * 512],
                                            start=(kk == 0), stop=(kk == NCH - 1))
                        mm = self.sT.inc(ins)
                        self.wait(a, mm)
                        self.slot_free[slot] = self.sA.inc(
                            self.A("activation", out=h1[j][:, ns * 512:(ns + 1) * 512],
                                   in_=self.pb[slot][:, :], func=AF.Gelu_apprx_tanh,
                                   bias=self.p_b1[:, j:j + 1]))
                w1_free[jg % 2] = self.sT.tok()
            h1_tok = self.sA.tok()
            for oc in range(NCH):
                for ns in range(TG // 512):
                    slot = 2 + (oc * 2 + ns) % 2
                    self.wait(te, w2tok)
                    self.wait(te, h1_tok)
                    self.wait(te, self.slot_free[slot])
                    for kk2 in range(F // 128):
                        ins = te.matmul(self.pb[slot][:, :], w2b[kk2][:, oc * 128:(oc + 1) * 128],
                                        h1[kk2][:, ns * 512:(ns + 1) * 512],
                                        start=(kk2 == 0), stop=(kk2 == F // 128 - 1))
                    mm = self.sT.inc(ins)
                    self.wait(v, mm)
                    self.slot_free[slot] = self.sV.inc(
                        self.V("scalar_tensor_tensor", out=s2[oc][:, ns * 512:(ns + 1) * 512],
                               in0=self.pb[slot][:, :], scalar=self.p_b2[:, oc:oc + 1],
                               in1=t_sl[oc][:, ns * 512:(ns + 1) * 512], op0=OP.add, op1=OP.add))
            s2_tok = self.sV.tok()
            if "h1d" in self.debug and tg == 0:
                h1t = self.dbg("h1d", [128, TG], BF16)
                self.wait(sy, h1_tok)
                hdt = self.dO.inc(sy.dma_start(out=h1t, in_=h1[0][:]), 16)
            if "s2d" in self.debug and tg == 0:
                s2t = self.dbg("s2d", [128, TG], BF16)
                self.wait(sy, s2_tok)
                sdt = self.dO.inc(sy.dma_start(out=s2t, in_=s2[0][:]), 16)
                self.wait(v, sdt)
            self.w2_free = self.sT.tok()
            self.t_free = self.sV.tok()
            self.dbg("attn", [D, T], BF16)
            self.ln_fm(s2, self.p_ln2s, self.p_ln2b, EPS_IN, s2_tok,
                       store=self.attn_scr, sig=tg, dbg_name="attn")
        st4.close()

    def p5_moe_final(self):
        nc = self.nc
        v, a, te, sy, gp = nc.vector, nc.scalar, nc.tensor, nc.sync, nc.gpsimd
        self.phase_barrier()
        st5 = contextlib.ExitStack()
        sb5 = lambda n, s, t: st5.enter_context(nc.sbuf_tensor(n, s, t))
        TG = 1024
        NK2 = I // 128
        xnm = [sb5(f"p5x{k}", [128, TG], BF16) for k in range(NCH)]
        h_e = [sb5(f"p5h{j}", [128, TG], BF16) for j in range(NK2)]
        y_acc = [sb5(f"p5y{k}", [128, TG], F32) for k in range(NCH)]
        gbuf = [sb5(f"p5g{b}", [128, 8 * 512], BF16) for b in range(2)]
        ubuf = [sb5(f"p5u{b}", [128, 8 * 512], BF16) for b in range(2)]
        w2b = [sb5(f"p5w2_{k}", [128, D], BF16) for k in range(NK2)]
        sbc = sb5("p5sbc", [128, TG], BF16)
        srow = sb5("p5srow", [1, TG], BF16)
        tmp1 = sb5("p5t1", [128, 512], F32)
        tmp2 = sb5("p5t2", [128, 512], F32)
        att = [sb5(f"p5a{k}", [128, TG], BF16) for k in range(NCH)]
        xld = [sb5(f"p5xl{b}", [128, D], F32) for b in range(2)]
        ost = [sb5(f"p5o{b}", [128, D], F32) for b in range(2)]
        gu_free = [None, None]
        w2_free = None
        yacc_free = None
        att_free = None
        pt_free = self.slot_free[7]
        ost_free = [None, None]
        htok = None
        for tg in range(T // TG):
            xtok = None
            for kk in range(NCH):
                self.wait(sy, yacc_free)
                xtok = self.dT.inc(sy.dma_start_transpose(
                    out=xnm[kk][:], in_=self.xn_scr[tg * TG:(tg + 1) * TG, kk * 128:(kk + 1) * 128]), 16)
            self.wait(v, yacc_free)
            for kk in range(NCH):
                self.V("memset", y_acc[kk][:], 0.0, dr=False)
            ymemset = self.sV.inc(self.V("memset", y_acc[0][:, 0:1], 0.0))
            for e in range(E):
                self.wait(sy, self.sV.tok()) if False else None
                self.wait(sy, getattr(self, "_srow_read", None))
                sr = self.dS.inc(sy.dma_start(out=srow[:],
                                              in_=self.s_scr[e:e + 1, tg * TG:(tg + 1) * TG]), 16)
                for ns in range(2):
                    self.wait(te, sr)
                    self.wait(te, self.slot_free[4])
                    bc = self.sT.inc(te.matmul(self.pb[4][:, :], self.c_onesrowb[:],
                                               srow[:, ns * 512:(ns + 1) * 512], start=True, stop=True))
                    self.wait(v, bc)
                    self.slot_free[4] = self.sV.inc(
                        self.V("tensor_copy", out=sbc[:, ns * 512:(ns + 1) * 512], in_=self.pb[4][:, :]))
                self._srow_read = self.sT.tok()
                sbctok = self.sV.tok()
                w2tok = None
                self.wait(gp, w2_free)
                for kk2 in range(NK2):
                    w2tok = self.dW.inc(gp.dma_start(
                        out=w2b[kk2][:], in_=self.w2m[e, kk2 * 128:(kk2 + 1) * 128, :]), 16)
                for jg in range(4):
                    wtok = None
                    self.wait(gp, gu_free[jg % 2])
                    for kk in range(NCH):
                        self.dW.inc(gp.dma_start(
                            out=gbuf[jg % 2][:, kk * 512:(kk + 1) * 512],
                            in_=self.w13[e, kk * 128:(kk + 1) * 128, jg * 512:(jg + 1) * 512]), 16)
                        wtok = self.dW.inc(gp.dma_start(
                            out=ubuf[jg % 2][:, kk * 512:(kk + 1) * 512],
                            in_=self.w13[e, kk * 128:(kk + 1) * 128, I + jg * 512:I + (jg + 1) * 512]), 16)
                    for j4 in range(4):
                        j = jg * 4 + j4
                        for ns in range(2):
                            self.wait(te, wtok)
                            self.wait(te, xtok)
                            self.wait(te, self.slot_free[0])
                            for kk in range(NCH):
                                ins = te.matmul(
                                    self.pb[0][:, :],
                                    gbuf[jg % 2][:, kk * 512 + j4 * 128:kk * 512 + (j4 + 1) * 128],
                                    xnm[kk][:, ns * 512:(ns + 1) * 512],
                                    start=(kk == 0), stop=(kk == NCH - 1))
                            mg = self.sT.inc(ins)
                            self.wait(te, self.slot_free[1])
                            for kk in range(NCH):
                                ins = te.matmul(
                                    self.pb[1][:, :],
                                    ubuf[jg % 2][:, kk * 512 + j4 * 128:kk * 512 + (j4 + 1) * 128],
                                    xnm[kk][:, ns * 512:(ns + 1) * 512],
                                    start=(kk == 0), stop=(kk == NCH - 1))
                            mu_ = self.sT.inc(ins)
                            self.wait(a, mg)
                            self.wait(a, htok)
                            t1tok = self.sA.inc(self.A("activation", out=tmp1[:],
                                                       in_=self.pb[0][:, :], func=AF.Silu))
                            self.slot_free[0] = t1tok
                            self.wait(v, mu_)
                            self.wait(v, sbctok)
                            t2tok = self.sV.inc(
                                self.V("tensor_tensor", out=tmp2[:], in0=sbc[:, ns * 512:(ns + 1) * 512],
                                       in1=self.pb[1][:, :], op=OP.mult))
                            self.slot_free[1] = t2tok
                            self.wait(v, t1tok)
                            htok = self.sV.inc(
                                self.V("tensor_tensor", out=h_e[j][:, ns * 512:(ns + 1) * 512],
                                       in0=tmp1[:], in1=tmp2[:], op=OP.mult))
                    gu_free[jg % 2] = self.sT.tok()
                he_tok = self.sV.tok()
                if "hed" in self.debug and tg == 0 and e == 0:
                    het = self.dbg("hed", [128, TG], BF16)
                    self.wait(sy, he_tok)
                    hetd = self.dO.inc(sy.dma_start(out=het, in_=h_e[0][:]), 16)
                    sbt = self.dbg("sbcd", [128, TG], BF16)
                    self.wait(sy, sbctok)
                    sbtd = self.dO.inc(sy.dma_start(out=sbt, in_=sbc[:]), 16)
                    self.wait(v, sbtd)
                    self.wait(self.nc.scalar, sbtd)
                for oc in range(NCH):
                    for ns in range(2):
                        slot = 2 + (oc * 2 + ns) % 2
                        self.wait(te, w2tok)
                        self.wait(te, he_tok)
                        self.wait(te, ymemset)
                        self.wait(te, self.slot_free[slot])
                        for kk2 in range(NK2):
                            ins = te.matmul(self.pb[slot][:, :], w2b[kk2][:, oc * 128:(oc + 1) * 128],
                                            h_e[kk2][:, ns * 512:(ns + 1) * 512],
                                            start=(kk2 == 0), stop=(kk2 == NK2 - 1))
                        mm_ = self.sT.inc(ins)
                        self.wait(v, mm_)
                        self.slot_free[slot] = self.sV.inc(
                            self.V("tensor_tensor", out=y_acc[oc][:, ns * 512:(ns + 1) * 512],
                                   in0=y_acc[oc][:, ns * 512:(ns + 1) * 512],
                                   in1=self.pb[slot][:, :], op=OP.add))
                w2_free = self.sT.tok()
            if "yacc" in self.debug and tg == 0:
                yt5 = self.dbg("yacc", [D, TG])
                self.wait(sy, self.sV.tok())
                for oc in range(NCH):
                    ydt5 = self.dO.inc(sy.dma_start(out=yt5[oc * 128:(oc + 1) * 128, :],
                                                    in_=y_acc[oc][:]), 16)
                self.wait(v, ydt5)
            attok = None
            for oc in range(NCH):
                self.wait(sy, att_free)
                attok = self.dS.inc(sy.dma_start(
                    out=att[oc][:], in_=self.attn_scr[oc * 128:(oc + 1) * 128, tg * TG:(tg + 1) * TG]), 16)
            self.wait(v, attok)
            for oc in range(NCH):
                self.V("tensor_tensor", out=att[oc][:], in0=att[oc][:], in1=y_acc[oc][:],
                       op=OP.add, dr=(oc == NCH - 1))
            sumtok = self.sV.inc(v.nop())
            yacc_free = sumtok
            for tau in range(TG // 128):
                self.wait(te, sumtok)
                self.wait(te, pt_free)
                self.wait(te, self.const_tok)
                for oc in range(NCH):
                    tt = te.transpose(self.pt[:, oc * 128:(oc + 1) * 128],
                                      att[oc][:, tau * 128:(tau + 1) * 128], self.c_identb[:])
                tttok = self.sT.inc(tt)
                b = tau % 2
                self.wait(sy, ost_free[b])
                xt_tok = self.dX.inc(sy.dma_start(
                    out=xld[b][:], in_=self.x[tg * TG + tau * 128:tg * TG + (tau + 1) * 128, :]), 16)
                self.wait(v, tttok)
                self.wait(v, xt_tok)
                self.wait(v, ost_free[b])
                otok = self.sV.inc(self.V("tensor_tensor", out=ost[b][:], in0=xld[b][:],
                                          in1=self.pt[:, 0:1024], op=OP.add))
                pt_free = otok
                self.wait(sy, otok)
                ost_free[b] = self.dO.inc(sy.dma_start(
                    out=self.out[tg * TG + tau * 128:tg * TG + (tau + 1) * 128, :], in_=ost[b][:]), 16)
            att_free = self.sT.tok()
        self.slot_free[7] = pt_free
        st5.close()

    def dump_s_scr(self):
        if "sdump" in self.debug:
            sd = self.dbg("sdump", [E, T], BF16)
            self.wait(self.nc.sync, self.dS.tok())
            self.dO.inc(self.nc.sync.dma_start(out=sd, in_=self.s_scr), 16)

    def dump_fm(self, name, tiles, sig):
        if name not in self.debug:
            return
        dbg = self.dbg_tensors.get(name)
        if dbg is None:
            dbg = self.dbg(name, [D, T], tiles[0].dtype)
        sy = self.nc.sync
        self.wait(sy, self.p1_done[0])
        self.wait(sy, self.p1_done[1])
        for c in range(NCH):
            self.dO.inc(sy.dma_start(
                out=dbg[c * 128:(c + 1) * 128, sig * S:(sig + 1) * S], in_=tiles[c][:]), 16)

    def finish(self):
        nc = self.nc
        self.wait(nc.sync, self.dO.tok())
        self.wait(nc.sync, self.dS.tok())
        self.stack.close()


def _host_constants():
    ident = np.eye(128)
    cm = np.ones((128, W), np.float32)
    cm[:, 0::L] = 0.0
    g4i = np.zeros((128, 4), np.float32)
    for p in range(128):
        g4i[p, p // 32] = 1.0
    cons = dict(
        identb=ident.astype(ml_dtypes.bfloat16),
        identf=ident.astype(np.float32),
        onesrow=np.ones((1, 128), np.float32),
        onesrowb=np.ones((1, 128), ml_dtypes.bfloat16),
        onescol=np.ones((128, 2), ml_dtypes.bfloat16),
        cmask=cm,
        g4ind=g4i,
        g4exp=np.ascontiguousarray(g4i.T),
        sconst=np.broadcast_to(np.array(
            [1e-5, 1e-6, -0.5, -1.0, 1e-30, 0.25, INV_2PI, 0.5], np.float32), (128, 8)).copy(),
    )
    return cons


PHASES = 5


def build(debug=(), phases=PHASES):
    b = Builder(debug)
    b.declare()
    b.load_constants()
    b.p0_outer_ln()
    if phases >= 1:
        b.alloc_sigma_bufs()
        for sig in range(NSEQ):
            b.p1_scan(sig)
            b.dump_fm("yr", b.yr_all, sig)
            if phases >= 3:
                b.p2_router(sig)
                b.p3_gn_wo_ln1(sig)
        if phases < 4:
            b.free_sigma_bufs()
    b.dump_s_scr()
    if phases >= 4:
        b.free_sigma_bufs()
        b.p4_ffn()
    if phases >= 5:
        b.p5_moe_final()
    b.finish()
    return b


def kernel(**inputs):
    b = build()
    return _run(b, inputs)


def _run(b, inputs, debug=()):
    cons = _host_constants()
    xfull = np.ascontiguousarray(np.asarray(inputs["L0_attn_x"], np.float32))
    in_maps = []
    for core in range(8):
        m = dict(
            x=np.ascontiguousarray(
                xfull[core * NSEQ:(core + 1) * NSEQ].reshape(T, D)),
            Wq=inputs["L0_attn_Wq"], Wk=inputs["L0_attn_Wk"], Wv=inputs["L0_attn_Wv"],
            Wa=inputs["L0_attn_Wa"], Wg=inputs["L0_attn_Wg"], Wo=inputs["L0_attn_Wo"],
            W1=inputs["L0_attn_W1"], W2=inputs["L0_attn_W2"],
            b1=inputs["L0_attn_b1"], b2=inputs["L0_attn_b2"],
            gns=inputs["L0_attn_gn_scale"], gnb=inputs["L0_attn_gn_bias"],
            ln1s=inputs["L0_attn_ln1_scale"], ln1b=inputs["L0_attn_ln1_bias"],
            ln2s=inputs["L0_attn_ln2_scale"], ln2b=inputs["L0_attn_ln2_bias"],
            rw=inputs["L0_ffn_router_w"], w13=inputs["L0_ffn_w13"], w2m=inputs["L0_ffn_w2"],
            **cons,
        )
        m = {k: np.ascontiguousarray(np.asarray(v)) for k, v in m.items()}
        in_maps.append(m)
    res = run_bass_kernel_spmd(b.nc, in_maps, core_ids=list(range(8)))
    outs = [r["out"] for r in res.results]
    full = np.concatenate(outs, axis=0).reshape(16, 2048, D).astype(np.float32)
    if debug:
        dbgs = {k: [r["dbg_" + k] for r in res.results] for k in debug}
        return full, dbgs
    return full
